# version 03 (final)
# baseline (speedup 1.0000x reference)
import numpy as np
import contextlib
import concourse.bass as bass
import concourse.mybir as mybir
from concourse.bass_utils import run_bass_kernel_spmd

F32 = mybir.dt.float32
BF16 = mybir.dt.bfloat16
FP8 = mybir.dt.float8e4
I8 = mybir.dt.int8
AF = mybir.ActivationFunctionType
ALU = mybir.AluOpType
AX = mybir.AxisListType

FMIN = float(np.finfo(np.float32).min)
SCL = 1.0 / float(np.sqrt(np.float32(128.0)))
LN_EPS = 1e-5
RMS_EPS = 1.1920929e-07

D = 2048
NT = 1024
S = 4096
INTER = 8192
GROUPS = [[0, 1, 2, 3], [4, 5, 6, 7]]


def build_nc():
    nc = bass.Bass("TRN2", target_bir_lowering=False, debug=False)

    I = {}
    def inp(name, shape, dt=F32):
        I[name] = nc.dram_tensor(name, list(shape), dt, kind="ExternalInput").ap()
        return I[name]

    inp("xq", (D, NT))
    inp("Wq_sl", (D, 512)); inp("Wk_sl", (D, 256)); inp("Wv_sl", (D, 256))
    inp("Wo_sl", (512, D))
    inp("Aq_t", (D, 32)); inp("Ak_t", (D, 32)); inp("Av_t", (D, 32))
    inp("Ao_sl", (512, 32))
    inp("Bq_sl", (32, 512)); inp("Bk_sl", (32, 256)); inp("Bv_sl", (32, 256))
    inp("Bo_t", (32, D))
    inp("sq", (1, 1)); inp("sk", (1, 1)); inp("sv", (1, 1)); inp("so", (1, 1))
    inp("qg_sl", (1, 4))
    inp("wg_i8", (D, INTER), I8); inp("wu_i8", (D, INTER), I8)
    inp("wd_i8", (INTER, D), I8)
    inp("sg_t", (128, 64)); inp("su_t", (128, 64)); inp("sd_t", (128, 16))
    inp("W1q_t", (D, D)); inp("W1k_t", (D, D)); inp("W1v_t", (D, D))
    inp("W1o_t", (D, D))
    inp("wg0", (D, INTER)); inp("wg1", (D, INTER)); inp("wl", (INTER, D))
    inp("ident", (128, 128))
    inp("permT0", (128, 128)); inp("permT1", (128, 128))
    inp("cos0", (64, S)); inp("sin0", (64, S))
    inp("cos1_my", (128, NT)); inp("sin1_my", (128, NT))
    inp("mask_d", (128, 2048))
    inp("mask_l1", (128, 2048))
    inp("hsel", (128, 4))

    out = nc.dram_tensor("out", [D, NT], F32, kind="ExternalOutput").ap()

    import os
    DBG = os.environ.get("KDBG", "") == "1"
    def dr(name, shape, dt=F32):
        kind = "ExternalOutput" if (DBG and name in (
            "xd", "q_scr", "k_scr", "vt_scr", "xn1_scr", "dbg1", "dbg2", "dbg3", "dbg4", "dbg5", "dbg6",
            "q1_scr", "k1_scr", "v1t_scr")) else "Internal"
        if kind == "ExternalOutput":
            return nc.dram_tensor(name, list(shape), dt, kind=kind).ap()
        return nc.dram_tensor(name, list(shape), dt).ap()
    xd = dr("xd", (D, NT))
    xn_ex = dr("xn_ex", (D, NT))
    xn_g = dr("xn_g", (4 * D, NT))
    ffh_scr = dr("ffh_scr", (INTER, NT), BF16)
    ffl_scr = dr("ffl_scr", (INTER, NT), BF16)
    q_scr = dr("q_scr", (512, S))
    k_scr = dr("k_scr", (256, S))
    vt_scr = dr("vt_scr", (S, 256))
    att_part = dr("att_part", (4 * D, NT))
    att_own = dr("att_own", (D, NT))
    xn1_scr = dr("xn1_scr", (D, NT))
    q1_scr = dr("q1_scr", (D, NT))
    k1_scr = dr("k1_scr", (D, NT))
    v1t_scr = dr("v1t_scr", (NT, D))
    hk_ex = dr("hk_ex", (D, 256))
    hv_ex = dr("hv_ex", (256, D))
    hk_g = dr("hk_g", (4 * D, 256))
    hv_g = dr("hv_g", (4 * 256, D))
    dbg1 = dr("dbg1", (D, NT)) if DBG else None
    dbg2 = dr("dbg2", (D, NT)) if DBG else None
    dbg3 = dr("dbg3", (128, 2048)) if DBG else None
    dbg4 = dr("dbg4", (D, NT)) if DBG else None
    dbg5 = dr("dbg5", (4 * D, NT)) if DBG else None
    dbg6 = dr("dbg6", (128, 4096)) if DBG else None

    es = contextlib.ExitStack()
    sbt = lambda name, shape, dt=F32: es.enter_context(nc.sbuf_tensor(name, list(shape), dt))

    A = sbt("A", (128, 8192))
    Bb = sbt("Bb", (128, 6144))
    C = sbt("C", (128, 4096))
    Dd = sbt("Dd", (128, 4096))
    bh = sbt("bh", (128, 8192), BF16)
    bl = sbt("bl", (128, 8192), BF16)
    bhf = bh[:].bitcast(F32)
    blf = bl[:].bitcast(F32)
    bh8 = bh[:].bitcast(FP8)
    bl8 = bl[:].bitcast(FP8)
    wb = sbt("wb", (128, 4096), BF16)
    wi8 = sbt("wi8", (128, 4096), I8)
    w8 = sbt("w8", (128, 4096), FP8)
    x8 = sbt("x8", (128, 8192), FP8)
    SW = [None, None]
    ffs = sbt("ffs", (128, 2048), BF16)
    SW[0] = x8[:].bitcast(F32)
    SW[1] = wb[:].bitcast(F32)
    ptb = sbt("ptb", (128, 512))
    row = sbt("row", (1, 2048))
    col = sbt("col", (128, 32))
    mk = sbt("mk", (128, 2048))
    s_id = sbt("s_id", (128, 128))
    s_p0 = sbt("s_p0", (128, 128))
    s_p1 = sbt("s_p1", (128, 128))
    onecol = sbt("onecol", (128, 1))
    ones1 = sbt("ones1", (1, 128))
    sgb = sbt("sgb", (128, 64)); sub = sbt("sub", (128, 64)); sdb = sbt("sdb", (128, 16))
    scb = sbt("scb", (128, 8))
    hsb = sbt("hsb", (128, 4))
    wAt = sbt("wAt", (128, 1536))
    Bsl = sbt("Bsl", (32, 2048))

    pb = [es.enter_context(nc.psum_tensor(f"pb{i}", [128, 512], F32)) for i in range(8)]

    sems = {}
    for sname in ("LD", "G", "T", "V", "SC", "CC"):
        sems[sname] = es.enter_context(nc.semaphore(sname))
    ctr = dict(LD=0, G=0, T=0, V=0, SC=0, CC=0)
    ops = []

    def wait(k, sem, n):
        assert n <= ctr[sem], f"wait {sem} {n} > {ctr[sem]}"
        if n > 0:
            ops.append((k, lambda e, s=sems[sem], v=n: e.wait_ge(s, v)))

    def dma(k, out_ap, in_ap):
        sem = "LD" if k == "sy" else "G"
        ctr[sem] += 16
        ops.append((k, lambda e, o=out_ap, i=in_ap, s=sems[sem]:
                    e.dma_start(out=o, in_=i).then_inc(s, 16)))
        return ctr[sem]

    def tmm(out_ap, l_ap, r_ap, start, stop, inc=False):
        if inc:
            ctr["T"] += 1
            ops.append(("t", lambda e, o=out_ap, l=l_ap, r=r_ap, a=start, b=stop, s=sems["T"]:
                        e.matmul(o, l, r, start=a, stop=b).then_inc(s, 1)))
        else:
            ops.append(("t", lambda e, o=out_ap, l=l_ap, r=r_ap, a=start, b=stop:
                        e.matmul(o, l, r, start=a, stop=b)))
        return ctr["T"]

    def ttr(out_ap, in_ap, inc=False):
        if inc:
            ctr["T"] += 1
            ops.append(("t", lambda e, o=out_ap, i=in_ap, s=sems["T"]:
                        e.transpose(o, i, s_id[:]).then_inc(s, 1)))
        else:
            ops.append(("t", lambda e, o=out_ap, i=in_ap: e.transpose(o, i, s_id[:])))
        return ctr["T"]

    def tw_v(n=None):
        wait("t", "V", ctr["V"] if n is None else n)

    def _v(fn):
        ctr["V"] += 1
        ops.append(("v", fn(sems["V"])))
        return ctr["V"]

    def vcopy(o, i):
        return _v(lambda s: (lambda e, O=o, I_=i: e.tensor_copy(out=O, in_=I_).then_inc(s, 1)))

    def vtt(o, a, b, op):
        return _v(lambda s: (lambda e, O=o, A_=a, B_=b, P=op: e.tensor_tensor(O, A_, B_, P).then_inc(s, 1)))

    def vts(o, a, sc, op):
        return _v(lambda s: (lambda e, O=o, A_=a, SC_=sc, P=op: e.tensor_single_scalar(O, A_, SC_, P).then_inc(s, 1)))

    def vtsp(o, a, sc_ap):
        return _v(lambda s: (lambda e, O=o, A_=a, SC_=sc_ap: e.tensor_scalar_mul(O, A_, SC_).then_inc(s, 1)))

    def vstt(o, in0, sc, in1, op0, op1):
        return _v(lambda s: (lambda e, O=o, A_=in0, SC_=sc, B_=in1, P0=op0, P1=op1:
                             e.scalar_tensor_tensor(O, A_, SC_, B_, P0, P1).then_inc(s, 1)))

    def vmax(o, i, negate=True):
        return _v(lambda s: (lambda e, O=o, I_=i, N=negate:
                             e.reduce_max(O, I_, axis=AX.X, negate=N).then_inc(s, 1)))

    def vrecip(o, i):
        return _v(lambda s: (lambda e, O=o, I_=i: e.reciprocal(O, I_).then_inc(s, 1)))

    def vmemset(a, v):
        return _v(lambda s: (lambda e, A_=a, V_=v: e.memset(A_, V_).then_inc(s, 1)))

    def sact(o, i, func, bias=0.0, scale=1.0, accum=None):
        ctr["SC"] += 1
        ops.append(("sc", lambda e, O=o, I_=i, F=func, B_=bias, SC_=scale, AC=accum, s=sems["SC"]:
                    e.activation(out=O, in_=I_, func=F, bias=B_, scale=SC_, accum_out=AC).then_inc(s, 1)))
        return ctr["SC"]

    def scopy(o, i):
        ctr["SC"] += 1
        ops.append(("sc", lambda e, O=o, I_=i, s=sems["SC"]: e.copy(out=O, in_=I_).then_inc(s, 1)))
        return ctr["SC"]

    def coll(kind, op, ins, outs_):
        ctr["CC"] += 1
        ops.append(("gp", lambda e, K=kind, O=op, II=ins, OO=outs_, s=sems["CC"]:
                    e.collective_compute(K, O, replica_groups=GROUPS, ins=II, outs=OO).then_inc(s, 1)))
        return ctr["CC"]

    def barrier():
        snap = dict(ctr)
        for k in ("sy", "gp", "t", "v", "sc"):
            for sem in ("LD", "G", "T", "V", "SC", "CC"):
                if snap[sem] > 0:
                    wait(k, sem, snap[sem])

    dma("sy", s_id[:], I["ident"])
    dma("sy", s_p0[:], I["permT0"])
    dma("sy", s_p1[:], I["permT1"])
    dma("sy", mk[:], I["mask_d"])
    dma("sy", sgb[:], I["sg_t"]); dma("sy", sub[:], I["su_t"]); dma("sy", sdb[:], I["sd_t"])
    dma("sy", scb[:, 0:1], I["sq"].broadcast_to([128, 1]))
    dma("sy", scb[:, 1:2], I["sk"].broadcast_to([128, 1]))
    dma("sy", scb[:, 2:3], I["sv"].broadcast_to([128, 1]))
    dma("sy", scb[:, 3:4], I["so"].broadcast_to([128, 1]))
    for hh in range(4):
        dma("sy", scb[:, 4 + hh:5 + hh], I["qg_sl"][:, hh:hh + 1].broadcast_to([128, 1]))
    n_const = dma("sy", hsb[:], I["hsel"])
    wait("v", "LD", n_const)
    vmemset(onecol[:], 1.0)
    vmemset(ones1[:], 1.0)
    vmemset(col[:, 8:9], LN_EPS)
    vmemset(col[:, 9:10], RMS_EPS)
    vmemset(col[:, 10:11], SCL)
    vmemset(col[:, 11:12], 1.0 / 128.0)

    def emit_layernorm(src_dram, dst_dram):
        for T in range(2):
            c0 = 512 * T
            wait("sy", "T", ctr["T"]); wait("sy", "V", ctr["V"])
            last = 0
            for dc in range(16):
                last = dma("sy", A[:, 512 * dc:512 * dc + 512],
                           src_dram[128 * dc:128 * dc + 128, c0:c0 + 512])
            wait("v", "LD", last)
            for dc in range(16):
                sl = 512 * (dc % 2)
                tw = ctr["T"]
                wait("v", "T", tw)
                vsq = vtt(Bb[:, sl:sl + 512], A[:, 512 * dc:512 * dc + 512],
                          A[:, 512 * dc:512 * dc + 512], ALU.mult)
                tw_v(vsq)
                tmm(pb[6][0:1, 0:512], onecol[:], A[:, 512 * dc:512 * dc + 512], dc == 0, dc == 15)
                tmm(pb[7][0:1, 0:512], onecol[:], Bb[:, sl:sl + 512], dc == 0, dc == 15,
                    inc=True)
            wait("v", "T", ctr["T"])
            vts(row[0:1, 0:512], pb[6][0:1, 0:512], -1.0 / 2048.0, ALU.mult)
            vts(row[0:1, 512:1024], pb[7][0:1, 0:512], 1.0 / 2048.0, ALU.mult)
            vtt(row[0:1, 1024:1536], row[0:1, 0:512], row[0:1, 0:512], ALU.mult)
            vv = vtt(row[0:1, 1024:1536], row[0:1, 512:1024], row[0:1, 1024:1536], ALU.subtract)
            wait("sc", "V", vv)
            ss = sact(row[0:1, 512:1024], row[0:1, 1024:1536], AF.Sqrt, bias=col[0:1, 8:9])
            wait("v", "SC", ss)
            vr = vrecip(row[0:1, 1024:1536], row[0:1, 512:1024])
            tw_v(vr)
            tmm(pb[4][:, 0:512], ones1[:], row[0:1, 0:512], True, True)
            tmm(pb[5][:, 0:512], ones1[:], row[0:1, 1024:1536], True, True, inc=True)
            wait("v", "T", ctr["T"])
            for dc in range(16):
                vtt(Bb[:, 1024:1536], A[:, 512 * dc:512 * dc + 512], pb[4][:, 0:512], ALU.add)
                vx = vtt(Bb[:, 1024:1536], Bb[:, 1024:1536], pb[5][:, 0:512], ALU.mult)
                wait("gp", "V", vx)
                g = dma("gp", dst_dram[128 * dc:128 * dc + 128, c0:c0 + 512], Bb[:, 1024:1536])
                wait("v", "G", g)
        return ctr["G"]

    g_xn = emit_layernorm(I["xq"], xn_ex)

    wait("gp", "G", g_xn)
    cc_xn = coll("AllGather", ALU.bypass, [xn_ex.opt()], [xn_g.opt()])

    for T in range(2):
        c0 = 512 * T
        barrier()
        last = 0
        for dc in range(16):
            last = dma("sy", A[:, 512 * dc:512 * dc + 512],
                       xn_ex[128 * dc:128 * dc + 128, c0:c0 + 512])
        wait("sc", "LD", last)
        for dc in range(16):
            scopy(bh[:, 512 * dc:512 * dc + 512], A[:, 512 * dc:512 * dc + 512])
        wait("v", "SC", ctr["SC"])
        for dc in range(16):
            vtt(bl[:, 512 * dc:512 * dc + 512], A[:, 512 * dc:512 * dc + 512],
                bh[:, 512 * dc:512 * dc + 512], ALU.subtract)
        v_split = ctr["V"]
        tw_v(v_split)
        for hc in range(64):
            wait("sy", "SC", ctr["SC"])
            dma("sy", wi8[:, 0:2048].rearrange("p (c f) -> p c f", f=128),
                I["wg_i8"][:, 128 * hc:128 * hc + 128].rearrange("(c p) f -> p c f", p=128))
            lw = dma("sy", wi8[:, 2048:4096].rearrange("p (c f) -> p c f", f=128),
                     I["wu_i8"][:, 128 * hc:128 * hc + 128].rearrange("(c p) f -> p c f", p=128))
            wait("sc", "LD", lw)
            wait("sc", "T", ctr["T"])
            scv = scopy(wb[:, 0:4096], wi8[:, 0:4096])
            wait("t", "SC", scv)
            bg, bu = pb[0], pb[1]
            for dc in range(16):
                lg = wb[:, 128 * dc:128 * dc + 128]
                lu = wb[:, 2048 + 128 * dc:2048 + 128 * dc + 128]
                rh = bh[:, 512 * dc:512 * dc + 512]
                rl = bl[:, 512 * dc:512 * dc + 512]
                tmm(bg[:, 0:512], lg, rh, dc == 0, False)
                tmm(bg[:, 0:512], lg, rl, False, dc == 15)
                tmm(bu[:, 0:512], lu, rh, dc == 0, False)
                tmm(bu[:, 0:512], lu, rl, False, dc == 15, inc=(dc == 15))
            wait("sc", "T", ctr["T"])
            wait("sc", "V", ctr["V"])
            ssl = sact(Bb[:, 4096:4608], bg[:, 0:512], AF.Silu, scale=sgb[:, hc:hc + 1])
            wait("v", "SC", ssl)
            wait("v", "G", ctr["G"])
            vfu = vstt(Bb[:, 4608:5120], bu[:, 0:512], sub[:, hc:hc + 1], Bb[:, 4096:4608],
                       ALU.mult, ALU.mult)
            wait("sc", "V", vfu)
            wait("sc", "G", ctr["G"])
            sfh = scopy(ffs[:, 0:512], Bb[:, 4608:5120])
            wait("v", "SC", sfh)
            vfl = vtt(ffs[:, 512:1024], Bb[:, 4608:5120], ffs[:, 0:512], ALU.subtract)
            wait("gp", "V", vfl)
            dma("gp", ffh_scr[128 * hc:128 * hc + 128, c0:c0 + 512], ffs[:, 0:512])
            dma("gp", ffl_scr[128 * hc:128 * hc + 128, c0:c0 + 512], ffs[:, 512:1024])
            if DBG and T == 0 and hc == 0:
                dma("gp", dbg3[:, 0:512], Bb[:, 4096:4608])
                dma("gp", dbg3[:, 512:1024], Bb[:, 4608:5120])
        for ep in range(2):
            barrier()
            for hc in range(64):
                wait("sy", "T", ctr["T"])
                dma("sy", ffs[:, 0:512].bitcast(BF16), ffh_scr[128 * hc:128 * hc + 128, c0:c0 + 512])
                lf = dma("sy", ffs[:, 512:1024].bitcast(BF16), ffl_scr[128 * hc:128 * hc + 128, c0:c0 + 512])
                lw = dma("sy", wi8[:, 0:1024], I["wd_i8"][128 * hc:128 * hc + 128, 1024 * ep:1024 * ep + 1024])
                wait("sc", "LD", lw)
                wait("sc", "T", ctr["T"])
                scv = scopy(wb[:, 0:1024], wi8[:, 0:1024])
                wait("t", "SC", scv)
                wait("t", "LD", lf)
                for ec in range(8):
                    lwt = wb[:, 128 * ec:128 * ec + 128]
                    tmm(pb[ec][:, 0:512], lwt, ffs[:, 0:512], hc == 0, False)
                    tmm(pb[ec][:, 0:512], lwt, ffs[:, 512:1024], False, hc == 63,
                        inc=(ec == 7))
            wait("v", "T", ctr["T"])
            wait("sy", "T", ctr["T"])
            for ec in range(8):
                eg = 8 * ep + ec
                lx = dma("sy", Bb[:, 0:512], I["xq"][128 * eg:128 * eg + 128, c0:c0 + 512])
                wait("v", "LD", lx)
                vx = vstt(Bb[:, 512:1024], pb[ec][:, 0:512], sdb[:, eg:eg + 1], Bb[:, 0:512],
                          ALU.mult, ALU.add)
                wait("gp", "V", vx)
                g = dma("gp", xd[128 * eg:128 * eg + 128, c0:c0 + 512], Bb[:, 512:1024])
                wait("v", "G", g)

    if DBG:
        barrier()
        dma("gp", dbg1, xd)
    barrier()
    dma("sy", C[0:64, 0:4096], I["cos0"])
    ldt = dma("sy", Dd[0:64, 0:4096], I["sin0"])
    for k in ("sy", "t", "v", "sc"):
        wait(k, "CC", cc_xn)
    wait("t", "LD", ldt)
    wait("v", "LD", ldt)

    for i, At in enumerate((I["Aq_t"], I["Ak_t"], I["Av_t"])):
        dma("sy", wAt[:, 512 * i:512 * i + 512].rearrange("p (c f) -> p c f", f=32),
            At.rearrange("(c p) f -> p c f", p=128))
    dma("sy", Bsl[:, 0:512], I["Bq_sl"])
    dma("sy", Bsl[:, 512:768], I["Bk_sl"])
    dma("sy", Bsl[:, 768:1024], I["Bv_sl"])
    swu = {0: 0, 1: 0}
    OCS = ([("q", lo, I["Wq_sl"], 0, 0) for lo in range(4)] +
           [("k", lo, I["Wk_sl"], 512, 1) for lo in range(2)] +
           [("v", lo, I["Wv_sl"], 768, 2) for lo in range(2)])

    for j in range(8):
        rnk, cj = j // 2, 512 * (j % 2)
        wait("sy", "T", ctr["T"])
        last = 0
        for dc in range(16):
            last = dma("sy", A[:, 512 * dc:512 * dc + 512],
                       xn_g[2048 * rnk + 128 * dc:2048 * rnk + 128 * dc + 128, cj:cj + 512])
        wait("t", "LD", last)
        for i in range(3):
            tw_v()
            for dc in range(16):
                tmm(pb[6][0:32, 0:512], wAt[:, 512 * i + 32 * dc:512 * i + 32 * dc + 32],
                    A[:, 512 * dc:512 * dc + 512], dc == 0, dc == 15, inc=(dc == 15))
            wait("v", "T", ctr["T"])
            vcopy(Bb[0:32, 2560 + 512 * i:3072 + 512 * i], pb[6][0:32, 0:512])
        xa = {"q": Bb[0:32, 2560:3072], "k": Bb[0:32, 3072:3584], "v": Bb[0:32, 3584:4096]}
        for oi, (kind, lo, Wsl, boff, sci) in enumerate(OCS):
            slot = oi % 2
            wait("sy", "T", swu[slot])
            ldw = dma("sy", SW[slot][:, 0:2048].rearrange("p (c f) -> p c f", f=128),
                      Wsl[:, 128 * lo:128 * lo + 128].rearrange("(c p) f -> p c f", p=128))
            tw_v()
            wait("t", "LD", ldw)
            bank, bank2 = pb[0], pb[1]
            for dc in range(16):
                tmm(bank[:, 0:512], SW[slot][:, 128 * dc:128 * dc + 128],
                    A[:, 512 * dc:512 * dc + 512], dc == 0, dc == 15)
            tmm(bank2[:, 0:512], Bsl[0:32, boff + 128 * lo:boff + 128 * lo + 128],
                xa[kind], True, True, inc=True)
            swu[slot] = ctr["T"]
            wait("v", "T", ctr["T"])
            stg = Bb[:, 0:512]
            vcopy(Bb[:, 1536:2048], bank2[:, 0:512])
            vstt(stg, bank[:, 0:512], scb[:, sci:sci + 1], Bb[:, 1536:2048], ALU.mult, ALU.add)
            if kind in ("q", "k"):
                vsq = vtt(Bb[:, 1024:1536], stg, stg, ALU.mult)
                tw_v(vsq)
                tmm(pb[6][0:1, 0:512], onecol[:], Bb[:, 1024:1536], True, True, inc=True)
                wait("sc", "T", ctr["T"])
                ssd = sact(row[0:1, 0:512], pb[6][0:1, 0:512], AF.Sqrt, bias=col[0:1, 9:10], scale=col[0:1, 11:12])
                wait("v", "SC", ssd)
                vr = vrecip(row[0:1, 512:1024], row[0:1, 0:512])
                tw_v(vr)
                tmm(pb[7][:, 0:512], ones1[:], row[0:1, 512:1024], True, True, inc=True)
                wait("v", "T", ctr["T"])
                vn = vtt(stg, stg, pb[7][:, 0:512], ALU.mult)
                tw_v(vn)
                tmm(pb[5][:, 0:512], s_p0[:], stg, True, True, inc=True)
                wait("v", "T", ctr["T"])
                vtt(Bb[0:64, 1024:1536], stg[0:64, :], C[0:64, 512 * j:512 * j + 512], ALU.mult)
                vtt(Bb[0:64, 1536:2048], pb[5][0:64, 0:512], Dd[0:64, 512 * j:512 * j + 512], ALU.mult)
                vtt(stg[0:64, :], Bb[0:64, 1024:1536], Bb[0:64, 1536:2048], ALU.add)
                if kind == "q":
                    vtsp(stg, stg, scb[:, 4 + lo:5 + lo])
            vd = ctr["V"]
            wait("gp", "V", vd)
            if kind == "q":
                g = dma("gp", q_scr[128 * lo:128 * lo + 128, 512 * j:512 * j + 512], stg)
                wait("v", "G", g)
            elif kind == "k":
                g = dma("gp", k_scr[128 * lo:128 * lo + 128, 512 * j:512 * j + 512], stg)
                wait("v", "G", g)
            else:
                tw_v(vd)
                for s4 in range(4):
                    ttr(pb[4][:, 128 * s4:128 * s4 + 128], stg[:, 128 * s4:128 * s4 + 128],
                        inc=(s4 == 3))
                wait("v", "T", ctr["T"])
                vtp = vcopy(Bb[:, 1024:1536], pb[4][:, 0:512])
                wait("gp", "V", vtp)
                g = 0
                for s4 in range(4):
                    g = dma("gp", vt_scr[512 * j + 128 * s4:512 * j + 128 * s4 + 128,
                                         128 * lo:128 * lo + 128],
                            Bb[:, 1024 + 128 * s4:1152 + 128 * s4])
                wait("v", "G", g)

    barrier()
    dma("sy", wAt[:, 0:128].rearrange("p (c f) -> p c f", f=32),
        I["Ao_sl"].rearrange("(c p) f -> p c f", p=128))
    dma("sy", Bsl[0:32, 0:2048], I["Bo_t"])
    KB = {0: C[:], 1: bhf}
    VB = {0: Dd[:], 1: blf}
    for qb in range(16):
        nkc = qb // 2 + 1
        nk = 512 * nkc
        wait("sy", "T", ctr["T"]); wait("sy", "V", ctr["V"])
        for h in range(4):
            dma("sy", Bb[:, 4096 + 256 * h:4096 + 256 * h + 256],
                q_scr[128 * h:128 * h + 128, 256 * qb:256 * qb + 256])
        for kvh in range(2):
            dma("sy", KB[kvh][:, 0:nk], k_scr[128 * kvh:128 * kvh + 128, 0:nk])
            dma("sy", VB[kvh][:, 0:nk].rearrange("p (c f) -> p c f", f=128),
                vt_scr[0:nk, 128 * kvh:128 * kvh + 128].rearrange("(c p) f -> p c f", p=128))
        ld_kv = ctr["LD"]
        wait("t", "LD", ld_kv)
        for h in range(4):
            kvh = h // 2
            qap = Bb[:, 4096 + 256 * h:4096 + 256 * h + 256]
            for ti in range(2):
                Sap = A[:, 4096 * ti:4096 * ti + nk]
                for jk in range(nkc):
                    tw_v()
                    tmm(pb[jk % 2][:, 0:512], qap[:, 128 * ti:128 * ti + 128],
                        KB[kvh][:, 512 * jk:512 * jk + 512], True, True, inc=True)
                    wait("v", "T", ctr["T"])
                    if jk == nkc - 1:
                        vstt(Sap[:, 512 * jk:512 * jk + 512], pb[jk % 2][:, 0:512], SCL,
                             mk[:, 512 * (2 * (qb % 2) + ti):512 * (2 * (qb % 2) + ti) + 512],
                             ALU.mult, ALU.add)
                    else:
                        vts(Sap[:, 512 * jk:512 * jk + 512], pb[jk % 2][:, 0:512], SCL, ALU.mult)
                dbgit = DBG and qb == 0 and h == 0 and ti == 1
                if dbgit:
                    g6 = dma("gp", dbg6[:, 0:512], Sap)
                    wait("v", "G", g6)
                vm = vmax(col[:, 0:1], Sap, negate=True)
                wait("sc", "V", vm)
                se = sact(Sap, Sap, AF.Exp, bias=col[:, 0:1], accum=col[:, 2:3])
                wait("v", "SC", se)
                if dbgit:
                    wait("gp", "SC", se)
                    dma("gp", dbg6[:, 512:1024], Sap)
                    g6 = dma("gp", dbg6[:, 1536:1552], col[:, 0:16])
                    wait("v", "G", g6)
                    wait("sc", "G", g6)
                vre = vrecip(col[:, 3:4], col[:, 2:3])
                wait("v", "V", vre)
                vtsp(Sap, Sap, col[:, 3:4])
                if dbgit:
                    wait("gp", "V", ctr["V"])
                    dma("gp", dbg6[:, 1024:1536], Sap)
                    g6 = dma("gp", dbg6[:, 1552:1568], col[:, 0:16])
                    wait("v", "G", g6)
                    wait("sc", "G", g6)
                    wait("t", "G", g6)
            v_sm = ctr["V"]
            tw_v(v_sm)
            for ks in range(nk // 128):
                bt = pb[4 + ks % 2]
                if ks >= 2:
                    tw_v()
                ttr(bt[:, 0:128], A[:, 128 * ks:128 * ks + 128])
                ttr(bt[:, 128:256], A[:, 4096 + 128 * ks:4096 + 128 * ks + 128], inc=True)
                wait("v", "T", ctr["T"])
                vpt = vcopy(ptb[:, 256 * (ks % 2):256 * (ks % 2) + 256], bt[:, 0:256])
                tw_v(vpt)
                tmm(pb[6][:, 0:256], VB[kvh][:, 128 * ks:128 * ks + 128],
                    ptb[:, 256 * (ks % 2):256 * (ks % 2) + 256],
                    ks == 0, ks == nk // 128 - 1, inc=(ks == nk // 128 - 1))
            wait("v", "T", ctr["T"])
            vy = vcopy(Bb[:, 256 * h:256 * h + 256], pb[6][:, 0:256])
            tw_v(vy)
        tw_v()
        for i in range(4):
            tmm(pb[7][0:32, 0:256], wAt[:, 32 * i:32 * i + 32],
                Bb[:, 256 * i:256 * i + 256], i == 0, i == 3, inc=(i == 3))
        wait("v", "T", ctr["T"])
        vxo = vcopy(Bb[0:32, 5632:5888], pb[7][0:32, 0:256])
        tw_v(vxo)
        for ec in range(16):
            bank, bank2 = pb[ec % 2], pb[2 + ec % 2]
            slot = ec % 2
            wait("sy", "T", swu[slot])
            ldw = dma("sy", SW[slot][:, 0:512].rearrange("p (c f) -> p c f", f=128),
                      I["Wo_sl"][:, 128 * ec:128 * ec + 128].rearrange("(c p) f -> p c f", p=128))
            if ec >= 2:
                tw_v()
            wait("t", "LD", ldw)
            for i in range(4):
                tmm(bank[:, 0:256], SW[slot][:, 128 * i:128 * i + 128],
                    Bb[:, 256 * i:256 * i + 256], i == 0, i == 3)
            tmm(bank2[:, 0:256], Bsl[0:32, 128 * ec:128 * ec + 128],
                Bb[0:32, 5632:5888], True, True, inc=True)
            swu[slot] = ctr["T"]
            wait("v", "T", ctr["T"])
            wait("v", "G", ctr["G"])
            vcopy(Bb[:, 5376:5632], bank2[:, 0:256])
            vo = vstt(Bb[:, 5120:5376], bank[:, 0:256], scb[:, 3:4], Bb[:, 5376:5632],
                      ALU.mult, ALU.add)
            wait("gp", "V", vo)
            dma("gp", att_part[2048 * (qb // 4) + 128 * ec:2048 * (qb // 4) + 128 * ec + 128,
                               256 * (qb % 4):256 * (qb % 4) + 256], Bb[:, 5120:5376])

    barrier()
    if DBG:
        dma("gp", dbg5, att_part)
        barrier()
    cc_rs = coll("ReduceScatter", ALU.add, [att_part.opt()], [att_own.opt()])
    if DBG:
        wait("gp", "CC", ctr["CC"])
        dma("gp", dbg4, att_own)
    for k in ("sy", "v"):
        wait(k, "CC", cc_rs)
    for dc in range(16):
        for T in range(2):
            c0 = 512 * T
            dma("sy", Bb[:, 0:512], att_own[128 * dc:128 * dc + 128, c0:c0 + 512])
            lx = dma("sy", Bb[:, 512:1024], xd[128 * dc:128 * dc + 128, c0:c0 + 512])
            wait("v", "LD", lx)
            va = vtt(Bb[:, 512:1024], Bb[:, 0:512], Bb[:, 512:1024], ALU.add)
            wait("gp", "V", va)
            g = dma("gp", xd[128 * dc:128 * dc + 128, c0:c0 + 512], Bb[:, 512:1024])
            wait("v", "G", g)
            wait("sy", "G", g)

    if DBG:
        barrier()
        dma("gp", dbg2, xd)
    barrier()
    emit_layernorm(xd, xn1_scr)

    barrier()
    dma("sy", C[:, 0:1024], I["cos1_my"])
    ldt = dma("sy", Dd[:, 0:1024], I["sin1_my"])
    wait("t", "LD", ldt); wait("v", "LD", ldt)
    for T in range(2):
        c0 = 512 * T
        wait("sy", "T", ctr["T"])
        last = 0
        for dc in range(16):
            last = dma("sy", A[:, 512 * dc:512 * dc + 512],
                       xn1_scr[128 * dc:128 * dc + 128, c0:c0 + 512])
        wait("t", "LD", last)
        for kind, Wt, dst in (("q", I["W1q_t"], q1_scr), ("k", I["W1k_t"], k1_scr),
                              ("v", I["W1v_t"], v1t_scr)):
            for hh in range(16):
                slot = hh % 2
                wait("sy", "T", swu[slot])
                ldw = dma("sy", SW[slot][:, 0:2048].rearrange("p (c f) -> p c f", f=128),
                          Wt[:, 128 * hh:128 * hh + 128].rearrange("(c p) f -> p c f", p=128))
                tw_v()
                wait("t", "LD", ldw)
                bank = pb[0]
                for dc in range(16):
                    tmm(bank[:, 0:512], SW[slot][:, 128 * dc:128 * dc + 128],
                        A[:, 512 * dc:512 * dc + 512], dc == 0, dc == 15, inc=(dc == 15))
                swu[slot] = ctr["T"]
                wait("v", "T", ctr["T"])
                stg = Bb[:, 0:512]
                vcopy(stg, bank[:, 0:512])
                if kind in ("q", "k"):
                    vv = ctr["V"]
                    tw_v(vv)
                    tmm(pb[5][:, 0:512], s_p1[:], stg, True, True, inc=True)
                    wait("v", "T", ctr["T"])
                    vtt(Bb[:, 1024:1536], stg, C[:, c0:c0 + 512], ALU.mult)
                    vtt(Bb[:, 1536:2048], pb[5][:, 0:512], Dd[:, c0:c0 + 512], ALU.mult)
                    vtt(stg, Bb[:, 1024:1536], Bb[:, 1536:2048], ALU.add)
                vd = ctr["V"]
                wait("gp", "V", vd)
                if kind in ("q", "k"):
                    g = dma("gp", dst[128 * hh:128 * hh + 128, c0:c0 + 512], stg)
                    if kind == "k" and T == 1:
                        g = dma("gp", hk_ex[128 * hh:128 * hh + 128, 0:256], stg[:, 256:512])
                    wait("v", "G", g)
                else:
                    tw_v(vd)
                    for s4 in range(4):
                        ttr(pb[4][:, 128 * s4:128 * s4 + 128], stg[:, 128 * s4:128 * s4 + 128],
                            inc=(s4 == 3))
                    wait("v", "T", ctr["T"])
                    vtp = vcopy(Bb[:, 1024:1536], pb[4][:, 0:512])
                    wait("gp", "V", vtp)
                    g = 0
                    for s4 in range(4):
                        g = dma("gp", dst[c0 + 128 * s4:c0 + 128 * s4 + 128,
                                          128 * hh:128 * hh + 128],
                                Bb[:, 1024 + 128 * s4:1152 + 128 * s4])
                        if T == 1 and s4 >= 2:
                            g = dma("gp", hv_ex[128 * (s4 - 2):128 * (s4 - 2) + 128,
                                               128 * hh:128 * hh + 128],
                                    Bb[:, 1024 + 128 * s4:1152 + 128 * s4])
                    wait("v", "G", g)

    barrier()
    coll("AllGather", ALU.bypass, [hk_ex.opt()], [hk_g.opt()])
    cc_h = coll("AllGather", ALU.bypass, [hv_ex.opt()], [hv_g.opt()])

    barrier()
    ldm = dma("sy", mk[:], I["mask_l1"])
    for k in ("sy", "v"):
        wait(k, "CC", cc_h)
    wait("v", "LD", ldm)
    for dc in range(16):
        last = 0
        for rp in range(4):
            last = dma("sy", Bb[:, 256 * rp:256 * rp + 256],
                       hk_g[2048 * rp + 128 * dc:2048 * rp + 128 * dc + 128, 0:256])
        wait("v", "LD", last)
        vtsp(C[:, 256 * dc:256 * dc + 256], Bb[:, 0:256], hsb[:, 0:1])
        for rp in range(1, 4):
            vstt(C[:, 256 * dc:256 * dc + 256], Bb[:, 256 * rp:256 * rp + 256],
                 hsb[:, rp:rp + 1], C[:, 256 * dc:256 * dc + 256], ALU.mult, ALU.add)
        wait("sy", "V", ctr["V"])
    for rt in range(2):
        last = 0
        for rp in range(4):
            last = dma("sy", A[:, 2048 * rp:2048 * rp + 2048],
                       hv_g[256 * rp + 128 * rt:256 * rp + 128 * rt + 128, 0:2048])
        wait("v", "LD", last)
        vtsp(Dd[:, 2048 * rt:2048 * rt + 2048], A[:, 0:2048], hsb[:, 0:1])
        for rp in range(1, 4):
            vstt(Dd[:, 2048 * rt:2048 * rt + 2048], A[:, 2048 * rp:2048 * rp + 2048],
                 hsb[:, rp:rp + 1], Dd[:, 2048 * rt:2048 * rt + 2048], ALU.mult, ALU.add)
        wait("sy", "V", ctr["V"])
    v_halo = ctr["V"]
    for c in range(4):
        for h in range(16):
            wait("sy", "T", ctr["T"])
            dma("sy", Bb[:, 4096:4224], q1_scr[128 * h:128 * h + 128, 256 * c:256 * c + 128])
            dma("sy", Bb[:, 4224:4352], q1_scr[128 * h:128 * h + 128, 256 * c + 128:256 * c + 256])
            if c == 0:
                lk = dma("sy", Bb[:, 4352:4608], k1_scr[128 * h:128 * h + 128, 0:256])
                nvt = 2
                lv = lk
                for s4 in range(2):
                    lv = dma("sy", Bb[:, 5376 + 128 * s4:5504 + 128 * s4],
                             v1t_scr[128 * s4:128 * s4 + 128, 128 * h:128 * h + 128])
            else:
                lk = dma("sy", Bb[:, 4352:4864], k1_scr[128 * h:128 * h + 128, 256 * c - 256:256 * c + 256])
                lv = lk
                for s4 in range(4):
                    lv = dma("sy", Bb[:, 5376 + 128 * s4:5504 + 128 * s4],
                             v1t_scr[256 * (c - 1) + 128 * s4:256 * (c - 1) + 128 * s4 + 128,
                                     128 * h:128 * h + 128])
            wait("t", "LD", lv)
            wait("t", "V", v_halo)
            for ti in range(2):
                tw_v()
                bank = pb[ti]
                qt = Bb[:, 4096 + 128 * ti:4224 + 128 * ti]
                if c == 0:
                    tmm(bank[:, 0:256], qt, C[:, 256 * h:256 * h + 256], True, True)
                    tmm(bank[:, 256:512], qt, Bb[:, 4352:4608], True, True, inc=True)
                else:
                    tmm(bank[:, 0:512], qt, Bb[:, 4352:4864], True, True, inc=True)
                wait("v", "T", ctr["T"])
                mvar = (0 if c == 0 else 2) + ti
                Sap = Bb[:, 4864 + 0:4864 + 512] if ti == 0 else A[:, 7168:7680]
                vstt(Sap, bank[:, 0:512], SCL, mk[:, 512 * mvar:512 * mvar + 512],
                     ALU.mult, ALU.add)
                vm = vmax(col[:, 0:1], Sap, negate=True)
                wait("sc", "V", vm)
                se = sact(Sap, Sap, AF.Exp, bias=col[:, 0:1], accum=col[:, 2:3])
                wait("v", "SC", se)
                vre = vrecip(col[:, 3:4], col[:, 2:3])
                wait("v", "V", vre)
                vtsp(Sap, Sap, col[:, 3:4])
            v_sm = ctr["V"]
            tw_v(v_sm)
            for ks in range(4):
                bt = pb[4 + ks % 2]
                if ks >= 2:
                    tw_v()
                ttr(bt[:, 0:128], Bb[:, 4864 + 128 * ks:4992 + 128 * ks])
                ttr(bt[:, 128:256], A[:, 7168 + 128 * ks:7296 + 128 * ks], inc=True)
                wait("v", "T", ctr["T"])
                vpt = vcopy(ptb[:, 256 * (ks % 2):256 * (ks % 2) + 256], bt[:, 0:256])
                tw_v(vpt)
                if c == 0 and ks < 2:
                    vtile = Dd[:, 2048 * ks + 128 * h:2048 * ks + 128 * h + 128]
                else:
                    vtile = Bb[:, 5376 + 128 * (ks if c != 0 else ks - 2):
                               5504 + 128 * (ks if c != 0 else ks - 2)]
                tmm(pb[6][:, 0:256], vtile, ptb[:, 256 * (ks % 2):256 * (ks % 2) + 256],
                    ks == 0, ks == 3, inc=(ks == 3))
            wait("v", "T", ctr["T"])
            vy = vcopy(Bb[:, 256 * h:256 * h + 256], pb[6][:, 0:256])
            tw_v(vy)
        for ec in range(16):
            bank = pb[ec % 2]
            slot = ec % 2
            wait("sy", "T", swu[slot])
            ldw = dma("sy", SW[slot][:, 0:2048].rearrange("p (c f) -> p c f", f=128),
                      I["W1o_t"][:, 128 * ec:128 * ec + 128].rearrange("(c p) f -> p c f", p=128))
            if ec >= 2:
                tw_v()
            wait("t", "LD", ldw)
            for dc in range(16):
                tmm(bank[:, 0:256], SW[slot][:, 128 * dc:128 * dc + 128],
                    Bb[:, 256 * dc:256 * dc + 256], dc == 0, dc == 15, inc=(dc == 15))
            swu[slot] = ctr["T"]
            wait("sy", "V", ctr["V"])
            lx = dma("sy", ptb[:, 0:256], xd[128 * ec:128 * ec + 128, 256 * c:256 * c + 256])
            wait("v", "T", ctr["T"])
            wait("v", "LD", lx)
            wait("v", "G", ctr["G"])
            vo = vtt(ptb[:, 256:512], bank[:, 0:256], ptb[:, 0:256], ALU.add)
            wait("gp", "V", vo)
            dma("gp", xd[128 * ec:128 * ec + 128, 256 * c:256 * c + 256], ptb[:, 256:512])

    for T in range(2):
        c0 = 512 * T
        barrier()
        last = 0
        for dc in range(16):
            last = dma("sy", A[:, 512 * dc:512 * dc + 512],
                       xn1_scr[128 * dc:128 * dc + 128, c0:c0 + 512])
        wait("v", "LD", last)
        for dc in range(16):
            vcopy(x8[:, 512 * dc:512 * dc + 512], A[:, 512 * dc:512 * dc + 512])
        v_q = ctr["V"]
        tw_v(v_q)
        for hc in range(64):
            wait("sy", "V", ctr["V"])
            dma("sy", Dd[:, 0:2048].rearrange("p (c f) -> p c f", f=128),
                I["wg0"][:, 128 * hc:128 * hc + 128].rearrange("(c p) f -> p c f", p=128))
            lw = dma("sy", Dd[:, 2048:4096].rearrange("p (c f) -> p c f", f=128),
                     I["wg1"][:, 128 * hc:128 * hc + 128].rearrange("(c p) f -> p c f", p=128))
            wait("v", "LD", lw)
            wait("v", "T", ctr["T"])
            vcv = vcopy(w8[:, 0:4096], Dd[:, 0:4096])
            tw_v(vcv)
            bg, bu = pb[0], pb[1]
            for dc in range(16):
                tmm(bg[:, 0:512], w8[:, 128 * dc:128 * dc + 128],
                    x8[:, 512 * dc:512 * dc + 512], dc == 0, dc == 15)
                tmm(bu[:, 0:512], w8[:, 2048 + 128 * dc:2048 + 128 * dc + 128],
                    x8[:, 512 * dc:512 * dc + 512], dc == 0, dc == 15, inc=(dc == 15))
            wait("sc", "T", ctr["T"])
            wait("sc", "V", ctr["V"])
            sg_ = scopy(ffs[:, 0:512], bg[:, 0:512])
            sgl = sact(ffs[:, 512:1024], ffs[:, 0:512], AF.Gelu_apprx_tanh)
            sf1 = scopy(ffs[:, 1024:1536], bu[:, 0:512])
            wait("v", "SC", sf1)
            vmu = vtt(ffs[:, 1536:2048], ffs[:, 512:1024], ffs[:, 1024:1536], ALU.mult)
            a8t = bh8[:, 512 * hc:512 * hc + 512] if hc < 32 else bl8[:, 512 * (hc - 32):512 * (hc - 32) + 512]
            vq8 = vcopy(a8t, ffs[:, 1536:2048])
        v_act = ctr["V"]
        tw_v(v_act)
        for ec in range(16):
            bank = pb[ec % 2]
            for half in range(2):
                wait("sy", "V", ctr["V"])
                last = dma("sy", Dd[:, 0:4096].rearrange("p (c f) -> p c f", f=128),
                           I["wl"][4096 * half:4096 * half + 4096,
                                   128 * ec:128 * ec + 128].rearrange("(c p) f -> p c f", p=128))
                wait("v", "LD", last)
                wait("v", "T", ctr["T"])
                vcv = vcopy(w8[:, 0:4096], Dd[:, 0:4096])
                tw_v(vcv)
                for hcl in range(32):
                    hc = 32 * half + hcl
                    a8t = bh8[:, 512 * hc:512 * hc + 512] if hc < 32 else bl8[:, 512 * (hc - 32):512 * (hc - 32) + 512]
                    tmm(bank[:, 0:512], w8[:, 128 * hcl:128 * hcl + 128],
                        a8t, hc == 0, hc == 63, inc=(hcl == 31))
            wait("sc", "T", ctr["T"])
            wait("sc", "V", ctr["V"])
            sfo = scopy(ffs[:, 0:512], bank[:, 0:512])
            wait("sy", "SC", sfo)
            lx = dma("sy", Bb[:, 0:512], xd[128 * ec:128 * ec + 128, c0:c0 + 512])
            wait("v", "SC", sfo)
            wait("v", "LD", lx)
            wait("v", "G", ctr["G"])
            vx = vtt(Bb[:, 512:1024], ffs[:, 0:512], Bb[:, 0:512], ALU.add)
            wait("gp", "V", vx)
            dma("gp", out[128 * ec:128 * ec + 128, c0:c0 + 512], Bb[:, 512:1024])

    g_final = ctr["G"]
    wait("gp", "G", g_final)
    wait("sy", "G", g_final)

    with es, nc.Block() as block:
        @block.sync
        def _(sync):
            for k, fn in ops:
                if k == "sy":
                    fn(sync)

        @block.vector
        def _(vector):
            for k, fn in ops:
                if k == "v":
                    fn(vector)

        @block.scalar
        def _(scalar):
            for k, fn in ops:
                if k == "sc":
                    fn(scalar)

        @block.tensor
        def _(tensor):
            for k, fn in ops:
                if k == "t":
                    fn(tensor)

        @block.gpsimd
        def _(gpsimd):
            for k, fn in ops:
                if k == "gp":
                    fn(gpsimd)

    return nc


_NC_CACHE = None


def _get_nc():
    global _NC_CACHE
    if _NC_CACHE is None:
        _NC_CACHE = build_nc()
    return _NC_CACHE


def _host_constants():
    pos = np.arange(S, dtype=np.float32)
    inv0 = (1.0 / 10000.0 ** (np.arange(0, 64, 2, dtype=np.float32) / 64.0)).astype(np.float32)
    fr0 = np.outer(pos, inv0).astype(np.float32)
    c0 = np.cos(fr0).astype(np.float32).T
    s0 = np.sin(fr0).astype(np.float32).T
    cos0 = np.concatenate([c0, c0], axis=0)
    sin0 = np.concatenate([s0, -s0], axis=0)
    inv1 = (1.0 / 10000.0 ** (np.arange(0, 128, 2, dtype=np.float32) / 128.0)).astype(np.float32)
    fr1 = np.outer(pos, inv1).astype(np.float32)
    c1 = np.cos(fr1).astype(np.float32).T
    s1 = np.sin(fr1).astype(np.float32).T
    cos1 = np.empty((128, S), dtype=np.float32)
    sin1 = np.empty((128, S), dtype=np.float32)
    cos1[0::2] = c1; cos1[1::2] = c1
    sin1[0::2] = -s1; sin1[1::2] = s1
    p0 = np.zeros((128, 128), dtype=np.float32)
    for i in range(32):
        p0[i, i + 32] = 1.0
        p0[i + 32, i] = 1.0
    for i in range(64, 128):
        p0[i, i] = 1.0
    p1 = np.zeros((128, 128), dtype=np.float32)
    for i in range(64):
        p1[2 * i, 2 * i + 1] = 1.0
        p1[2 * i + 1, 2 * i] = 1.0
    permT0 = np.ascontiguousarray(p0.T)
    permT1 = np.ascontiguousarray(p1.T)
    ident = np.eye(128, dtype=np.float32)
    r = np.arange(128)[:, None]
    cc = np.arange(512)[None, :]
    mask_d = np.concatenate(
        [np.where(cc <= d_ + r, 0.0, FMIN).astype(np.float32) for d_ in (0, 128, 256, 384)],
        axis=1)
    def l1mask(first, ti):
        m = np.zeros((128, 512), dtype=np.float32)
        own = cc - 256
        allow_own = own <= (128 * ti + r)
        m[:, 256:] = np.where(allow_own[:, 256:], 0.0, FMIN)
        if first:
            m[:, :256] = FMIN
        return m
    mask_c0 = [l1mask(True, 0), l1mask(True, 1)]
    mask_b = [l1mask(False, 0), l1mask(False, 1)]
    return dict(cos0=cos0, sin0=sin0, cos1=cos1, sin1=sin1, permT0=permT0,
                permT1=permT1, ident=ident, mask_d=mask_d, mask_c0=mask_c0, mask_b=mask_b)


def _build_in_maps(inputs):
    cst = _host_constants()
    x = np.asarray(inputs["L0_attn_x"], dtype=np.float32)
    WqT = np.ascontiguousarray(np.asarray(inputs["L0_attn_W_q"]).T)
    WkT = np.ascontiguousarray(np.asarray(inputs["L0_attn_W_k"]).T)
    WvT = np.ascontiguousarray(np.asarray(inputs["L0_attn_W_v"]).T)
    WoT = np.ascontiguousarray(np.asarray(inputs["L0_attn_W_o"]).T)
    AqT = np.ascontiguousarray(np.asarray(inputs["L0_attn_A_q"]).T)
    AkT = np.ascontiguousarray(np.asarray(inputs["L0_attn_A_k"]).T)
    AvT = np.ascontiguousarray(np.asarray(inputs["L0_attn_A_v"]).T)
    AoT = np.ascontiguousarray(np.asarray(inputs["L0_attn_A_o"]).T)
    BqT = np.ascontiguousarray(np.asarray(inputs["L0_attn_B_q"]).T)
    BkT = np.ascontiguousarray(np.asarray(inputs["L0_attn_B_k"]).T)
    BvT = np.ascontiguousarray(np.asarray(inputs["L0_attn_B_v"]).T)
    BoT = np.ascontiguousarray(np.asarray(inputs["L0_attn_B_o"]).T)
    sq = np.asarray(inputs["L0_attn_s_q"], np.float32).reshape(1, 1)
    sk = np.asarray(inputs["L0_attn_s_k"], np.float32).reshape(1, 1)
    sv = np.asarray(inputs["L0_attn_s_v"], np.float32).reshape(1, 1)
    so = np.asarray(inputs["L0_attn_s_o"], np.float32).reshape(1, 1)
    qg = np.asarray(inputs["L0_attn_q_gain"], np.float32).reshape(1, 16)
    wg_i8 = np.ascontiguousarray(np.asarray(inputs["L0_ffn_w_gate"], np.int8))
    wu_i8 = np.ascontiguousarray(np.asarray(inputs["L0_ffn_w_up"], np.int8))
    wd_i8 = np.ascontiguousarray(np.asarray(inputs["L0_ffn_w_down"], np.int8))
    sg_t = np.ascontiguousarray(np.asarray(inputs["L0_ffn_s_gate"], np.float32).reshape(64, 128).T)
    su_t = np.ascontiguousarray(np.asarray(inputs["L0_ffn_s_up"], np.float32).reshape(64, 128).T)
    sd_t = np.ascontiguousarray(np.asarray(inputs["L0_ffn_s_down"], np.float32).reshape(16, 128).T)
    W1qT = np.ascontiguousarray(np.asarray(inputs["L1_attn_Wq"]).T)
    W1kT = np.ascontiguousarray(np.asarray(inputs["L1_attn_Wk"]).T)
    W1vT = np.ascontiguousarray(np.asarray(inputs["L1_attn_Wv"]).T)
    W1oT = np.ascontiguousarray(np.asarray(inputs["L1_attn_Wo"]).T)
    wg = np.asarray(inputs["L1_ffn_w_gating"], np.float32)
    wg0 = np.ascontiguousarray(wg[0])
    wg1 = np.ascontiguousarray(wg[1])
    wl = np.ascontiguousarray(np.asarray(inputs["L1_ffn_w_linear"], np.float32))

    in_maps = []
    for c in range(8):
        b, g = c // 4, c % 4
        m = {}
        m["xq"] = np.ascontiguousarray(x[b, NT * g:NT * (g + 1), :].T)
        m["Wq_sl"] = np.ascontiguousarray(WqT[:, 512 * g:512 * (g + 1)])
        m["Wk_sl"] = np.ascontiguousarray(WkT[:, 256 * g:256 * (g + 1)])
        m["Wv_sl"] = np.ascontiguousarray(WvT[:, 256 * g:256 * (g + 1)])
        m["Wo_sl"] = np.ascontiguousarray(WoT[512 * g:512 * (g + 1), :])
        m["Aq_t"] = AqT; m["Ak_t"] = AkT; m["Av_t"] = AvT
        m["Ao_sl"] = np.ascontiguousarray(AoT[512 * g:512 * (g + 1), :])
        m["Bq_sl"] = np.ascontiguousarray(BqT[:, 512 * g:512 * (g + 1)])
        m["Bk_sl"] = np.ascontiguousarray(BkT[:, 256 * g:256 * (g + 1)])
        m["Bv_sl"] = np.ascontiguousarray(BvT[:, 256 * g:256 * (g + 1)])
        m["Bo_t"] = BoT
        m["sq"] = sq; m["sk"] = sk; m["sv"] = sv; m["so"] = so
        m["qg_sl"] = np.ascontiguousarray(qg[:, 4 * g:4 * (g + 1)])
        m["wg_i8"] = wg_i8; m["wu_i8"] = wu_i8; m["wd_i8"] = wd_i8
        m["sg_t"] = sg_t; m["su_t"] = su_t; m["sd_t"] = sd_t
        m["W1q_t"] = W1qT; m["W1k_t"] = W1kT; m["W1v_t"] = W1vT; m["W1o_t"] = W1oT
        m["wg0"] = wg0; m["wg1"] = wg1; m["wl"] = wl
        m["ident"] = cst["ident"]; m["permT0"] = cst["permT0"]; m["permT1"] = cst["permT1"]
        m["cos0"] = cst["cos0"]; m["sin0"] = cst["sin0"]
        m["cos1_my"] = np.ascontiguousarray(cst["cos1"][:, NT * g:NT * (g + 1)])
        m["sin1_my"] = np.ascontiguousarray(cst["sin1"][:, NT * g:NT * (g + 1)])
        m["mask_d"] = cst["mask_d"]
        if g == 0:
            m["mask_l1"] = np.concatenate(cst["mask_c0"] + cst["mask_b"], axis=1)
        else:
            m["mask_l1"] = np.concatenate(cst["mask_b"] + cst["mask_b"], axis=1)
        hs = np.zeros((128, 4), dtype=np.float32)
        if g > 0:
            hs[:, g - 1] = 1.0
        m["hsel"] = hs
        in_maps.append(m)
    return in_maps


def kernel(**inputs):
    in_maps = _build_in_maps(inputs)
    nc = _get_nc()
    res = run_bass_kernel_spmd(nc, in_maps, core_ids=list(range(8)))
    outp = np.empty((2, 4096, 2048), dtype=np.float32)
    for c in range(8):
        b, g = c // 4, c % 4
        outp[b, NT * g:NT * (g + 1), :] = np.asarray(res.results[c]["out"]).T
    return outp


def prep_in_maps(inputs):
    return _build_in_maps(inputs)


if __name__ == "__main__":
    nc = build_nc()
    print("built ok")
